# version 04
# baseline (speedup 1.0000x reference)
import numpy as np
import ml_dtypes
from contextlib import ExitStack

import concourse.bass as bass
import concourse.mybir as mybir
from concourse.bass_utils import run_bass_kernel_spmd

F32 = mybir.dt.float32
BF16 = mybir.dt.bfloat16
AF = mybir.ActivationFunctionType
ALU = mybir.AluOpType
AX = mybir.AxisListType

MAGIC = 12582912.0
SEQ = 8192
TC = 2048
NT = 16
D = 1024
DCH = 8
C = 4096
CCH = 32
PW = 512
NPAN = 4
JT = 64
NXT1 = 18
NXC1 = NXT1 * 128
NXC2 = TC + 2
KVN = DCH * 128 * TC + TC * D
VBASE = DCH * 128 * TC

_TAPS = ()
_DBG = None


def _build(taps=(), stop_after=None):
    nc = bass.Bass("TRN2", target_bir_lowering=False)
    ap = lambda t: t.ap()

    x_in = ap(nc.dram_tensor("x_in", [NXC1, D], F32, kind="ExternalInput"))
    w1T = ap(nc.dram_tensor("w1T", [D, C], F32, kind="ExternalInput"))
    w2T = ap(nc.dram_tensor("w2T", [C, D], F32, kind="ExternalInput"))
    wqkvT = ap(nc.dram_tensor("wqkvT", [D, 3 * D], F32, kind="ExternalInput"))
    bcol_d = ap(nc.dram_tensor("bcol", [128, 24], F32, kind="ExternalInput"))
    bvrow_d = ap(nc.dram_tensor("bvrow", [1, D], F32, kind="ExternalInput"))
    cwcol_d = ap(nc.dram_tensor("cwcol", [128, 96], F32, kind="ExternalInput"))
    cbcol_d = ap(nc.dram_tensor("cbcol", [128, 32], F32, kind="ExternalInput"))
    identb_d = ap(nc.dram_tensor("identb", [128, 128], BF16, kind="ExternalInput"))
    identf_d = ap(nc.dram_tensor("identf", [128, 128], F32, kind="ExternalInput"))
    ones128_d = ap(nc.dram_tensor("ones128", [128, 128], F32, kind="ExternalInput"))
    selmat_d = ap(nc.dram_tensor("selmat", [128, 2], F32, kind="ExternalInput"))
    consts_d = ap(nc.dram_tensor("consts", [128, 4], F32, kind="ExternalInput"))
    y_out = ap(nc.dram_tensor("y_out", [TC, D], F32, kind="ExternalOutput"))

    wq1 = ap(nc.dram_tensor("wq1", [DCH, 128, C], BF16))
    wq2 = ap(nc.dram_tensor("wq2", [CCH, 128, D], BF16))
    x2q = ap(nc.dram_tensor("x2q", [NT, 128, CCH, 128], BF16))
    pst = ap(nc.dram_tensor("pst", [JT, 4, 128, 4, 128], BF16))
    kv_in = ap(nc.dram_tensor("kv_in", [1, KVN], BF16))
    kv_out = ap(nc.dram_tensor("kv_out", [4, KVN], BF16))
    z_in = ap(nc.dram_tensor("z_in", [1, SEQ], F32))
    z_out = ap(nc.dram_tensor("z_out", [1, SEQ], F32))
    hx_in = ap(nc.dram_tensor("hx_in", [2, D], F32))
    hx_out = ap(nc.dram_tensor("hx_out", [16, D], F32))
    x2d = ap(nc.dram_tensor("x2d", [TC, D], F32))
    tap_aps = {t: ap(nc.dram_tensor("tap_" + t, [TC, D], F32, kind="ExternalOutput"))
               for t in taps}
    dbg = _DBG and ap(nc.dram_tensor("dbg", [128, 36864], F32, kind="ExternalOutput"))

    with ExitStack() as ctx:
        sbuf = lambda n, sh, dt: ctx.enter_context(nc.sbuf_tensor(n, sh, dt))

        identb = sbuf("identb_s", [128, 128], BF16)
        identf = sbuf("identf_s", [128, 128], F32)
        ones128 = sbuf("ones128_s", [128, 128], F32)
        browbuf = sbuf("browbuf_s", [128, D], F32)
        consts = sbuf("consts_s", [128, 4], F32)
        bcol = sbuf("bcol_s", [128, 24], F32)
        cw = sbuf("cw_s", [128, 96], F32)
        cb = sbuf("cb_s", [128, 32], F32)
        selmat = sbuf("selmat_s", [128, 2], F32)
        swq1 = sbuf("swq1", [128, 1], F32)
        swi1 = sbuf("swi1", [128, 1], F32)
        swq2 = sbuf("swq2", [128, 1], F32)
        sch2 = sbuf("sch2", [128, 1], F32)
        r2cols = sbuf("r2cols", [128, NT], F32)
        sccols = sbuf("sccols", [128, NT], F32)
        Zbuf = sbuf("Zbuf_s", [128, JT], F32)
        gapc = sbuf("gapc_s", [128, 1], F32)
        magic = consts[:, 0:1]

        class Ctr:
            def __init__(self, name):
                self.sem = ctx.enter_context(nc.semaphore(name))
                self.n = 0

        HOLD = {}
        dS, dA, dG = Ctr("dS"), Ctr("dA"), Ctr("dG")
        cT, cV, cS, cG, cC = Ctr("cT"), Ctr("cV"), Ctr("cS"), Ctr("cG"), Ctr("cC")

        block = ctx.enter_context(nc.Block())

        class _Rec:
            def __init__(self, e):
                self._e = e
                self.last = None

            def __getattr__(self, name):
                f = getattr(self._e, name)
                if not callable(f):
                    return f

                def wrap(*a, **k):
                    r = f(*a, **k)
                    if hasattr(r, "then_inc"):
                        self.last = r
                    return r
                return wrap

        def grp(eng, fn, w=(), ctr="auto"):
            c = {"tensor": cT, "vector": cV, "scalar": cS,
                 "gpsimd": cG, "sync": None}[eng] if ctr == "auto" else ctr
            for (cc, v) in w:
                assert v <= cc.n, (eng, v, cc.n)

            def body(e):
                for (cc, v) in w:
                    if v > 0:
                        e.wait_ge(cc.sem, v)
                rec = _Rec(e)
                fn(rec)
                if c is not None:
                    assert rec.last is not None, eng
                    rec.last.then_inc(c.sem, 1)
            getattr(block, eng)(body)
            if c is not None:
                c.n += 1
                return c.n

        def dma(eng, q, out, in_, w=()):
            for (cc, v) in w:
                assert v <= cc.n, (eng, v, cc.n)

            def body(e):
                for (cc, v) in w:
                    if v > 0:
                        e.wait_ge(cc.sem, v)
                e.dma_start(out=out, in_=in_).then_inc(q.sem, 16)
            getattr(block, eng)(body)
            q.n += 16
            return q.n

        for dst, src in [(identb[:, :], identb_d), (identf[:, :], identf_d),
                         (ones128[:, :], ones128_d), (consts[:, :], consts_d),
                         (bcol[:, :], bcol_d), (cw[:, :], cwcol_d),
                         (cb[:, :], cbcol_d), (selmat[:, :], selmat_d)]:
            dma("gpsimd", dG, dst, src)
        statics_in = dG.n
        bb_zero = grp("vector", lambda v: v.memset(browbuf[:, :], 0.0), w=[])

        with nc.sbuf_tensor("wtmp", [128, 2, C], F32) as wtmp, \
             nc.sbuf_tensor("qtmp", [128, C], F32) as qtmp, \
             nc.sbuf_tensor("wstage", [128, 2, C], BF16) as wstage, \
             nc.sbuf_tensor("abscols", [128, 32], F32) as abscols, \
             nc.sbuf_tensor("arow", [1, 128], F32) as arow, \
             nc.psum_tensor("p0c", [128, 1], F32) as p0c, \
             nc.psum_tensor("p0r", [1, 128], F32) as p0r:
            for wi, (wsrc, nch, wid, wqd, sqcol, sicol) in enumerate(
                    [(w1T, DCH, C, wq1, swq1, swi1),
                     (w2T, CCH, D, wq2, swq2, sch2)]):
                abs_done = {}
                for k in range(nch):
                    buf = wtmp[:, k % 2, 0:wid]
                    w_ = [(cS, abs_done[k - 2])] if k >= 2 else [(cS, cS.n)]
                    need = dma("gpsimd", dG, buf, wsrc[k * 128:(k + 1) * 128, 0:wid], w=w_)
                    abs_done[k] = grp("scalar", lambda s, b=buf, k=k, wid=wid: s.activation(
                        qtmp[:, 0:wid], b, AF.Abs, accum_out=abscols[:, k:k + 1]),
                        w=[(dG, need)])
                vfin = grp("vector", lambda v, nch=nch: v.reduce_sum(
                    abscols[:, 0:1], abscols[:, 0:nch], axis=AX.X),
                    w=[(cS, abs_done[nch - 1])])
                tfin = grp("tensor", lambda t: t.transpose(
                    p0r[0:1, 0:128], abscols[:, 0:1], identf[:, :]),
                    w=[(cV, vfin), (dG, statics_in)])
                sfin = grp("scalar", lambda s: s.copy(arow[0:1, :], p0r[0:1, 0:128]),
                           w=[(cT, tfin)])

                def tot(v, nch=nch, wid=wid):
                    v.reduce_sum(browbuf[0:1, 0:1], arow[0:1, :], axis=AX.X)
                    v.tensor_copy(gapc[:, :], consts[:, 1:2])
                    v.tensor_scalar(browbuf[0:1, 0:1], browbuf[0:1, 0:1],
                                    float(1.0 / (nch * 128 * wid)), 1e-5, ALU.mult, ALU.max)
                    v.tensor_copy(gapc[:, :], consts[:, 1:2])
                    v.reciprocal(browbuf[0:1, 1:2], browbuf[0:1, 0:1])
                vtot = grp("vector", tot, w=[(cS, sfin)])
                t1 = grp("tensor", lambda t: t.matmul(p0c[:, :], ones128[:, :],
                                                      browbuf[:, 1:2], start=True, stop=True),
                         w=[(cV, vtot)])
                s1 = grp("scalar", lambda s, q=sqcol: s.copy(q[:, :], p0c[:, :]), w=[(cT, t1)])
                t2 = grp("tensor", lambda t: t.matmul(p0c[:, :], ones128[:, :],
                                                      browbuf[:, 0:1], start=True, stop=True),
                         w=[(cS, s1)])
                s2 = grp("scalar", lambda s, q=sicol: s.copy(q[:, :], p0c[:, :]), w=[(cT, t2)])
                if wi == 1:
                    grp("vector", lambda v: v.tensor_scalar_mul(sch2[:, :], sch2[:, :], 0.5),
                        w=[(cS, s2)])
                q_done = {}
                for k in range(nch):
                    buf = wtmp[:, k % 2, 0:wid]
                    w_ = [(cS, q_done[k - 2])] if k >= 2 else [(cS, s2)]
                    need = dma("gpsimd", dG, buf, wsrc[k * 128:(k + 1) * 128, 0:wid], w=w_)
                    wst = [(dG, need)]
                    wst.append((dA, dA.n - 16 if k >= 2 else dA.n))
                    q_done[k] = grp("scalar", lambda s, b=buf, wid=wid, q=sqcol: s.activation(
                        qtmp[:, 0:wid], b, AF.Identity, bias=magic, scale=q[:, :]), w=wst)
                    stg = wstage[:, k % 2, 0:wid]

                    def qv(v, stg=stg, wid=wid):
                        v.tensor_scalar(qtmp[:, 0:wid], qtmp[:, 0:wid], -MAGIC, 1.0,
                                        ALU.add, ALU.min)
                        v.tensor_scalar(stg, qtmp[:, 0:wid], -1.0, None, ALU.max)
                    vq = grp("vector", qv, w=[(cS, q_done[k])])
                    dma("scalar", dA, wqd[k, :, :], stg, w=[(cV, vq)])
        wq1_ready = DCH * 16
        wq2_ready = (DCH + CCH) * 16
        if stop_after == "ph0":
            return nc

        def ffn_sweep1(tag, nxc, n_main, src_row, src_wait, tile_col, has_mini,
                       panel_base, start_wait=(), stop=None):
            with nc.sbuf_tensor(tag + "w1sb", [128, DCH * C], BF16) as w1sb, \
                 nc.sbuf_tensor(tag + "XT", [128, DCH * nxc], BF16) as XT, \
                 nc.sbuf_tensor(tag + "hp", [128, CCH, PW + 2], F32) as hp, \
                 nc.sbuf_tensor(tag + "xb", [128, 2, D], F32) as xb, \
                 nc.sbuf_tensor(tag + "xln", [128, 2, D], F32) as xln, \
                 nc.sbuf_tensor(tag + "xq", [128, 2, D], BF16) as xq, \
                 nc.sbuf_tensor(tag + "stc", [128, 8, 2], F32) as stc, \
                 nc.sbuf_tensor(tag + "cvt", [128, PW], F32) as cvt, \
                 nc.sbuf_tensor(tag + "Rmx", [128, PW], F32) as Rmx, \
                 nc.sbuf_tensor(tag + "Mt", [128, 2, 128], F32) as Mt, \
                 nc.sbuf_tensor(tag + "S2B", [128, PW], F32) as S2B, \
                 nc.sbuf_tensor(tag + "rnd", [128, 2, PW], F32) as rnd, \
                 nc.sbuf_tensor(tag + "q2s", [128, 2, PW], BF16) as q2s, \
                 nc.psum_tensor(tag + "pHa", [128, 2, 512], F32) as pHa, \
                 nc.psum_tensor(tag + "pHb", [128, 2, 512], F32) as pHb, \
                 nc.psum_tensor(tag + "pT", [128, 2, 128], F32) as pT, \
                 nc.psum_tensor(tag + "pB", [128, 512], F32) as pB:

                for k in range(DCH):
                    w1_in = dma("sync", dS, w1sb[:, k * C:(k + 1) * C],
                                wq1[k, :, :],
                                w=([(dA, wq1_ready)] + list(start_wait)) if k == 0 else [])

                tiles = ([("mini", 0)] if has_mini else []) + \
                        [("main", t) for t in range(n_main)]
                v5_of, ev_of, tp_of = {}, {}, {}

                class M:
                    pass

                def emit_quant(members):
                    solo = len(members) == 1
                    ms = []
                    for (idx, kind, t) in members:
                        m = M()
                        m.idx, m.kind, m.t = idx, kind, t
                        m.slot = idx % 2
                        m.np = 128 if kind == "main" else 2
                        if kind == "main":
                            m.xbuf = xb[0:128, m.slot, :]
                            w_ = list(start_wait) + src_wait(t)
                            if idx - 2 in v5_of:
                                w_.append((cV, v5_of[idx - 2]))
                            m.need_x = dma("sync", dS, m.xbuf, src_row(t), w=w_)
                        else:
                            m.xbuf = HOLD["xm2"][0:2, :]
                            m.need_x = None
                        m.xlnb = xln[0:m.np, m.slot, :]
                        m.rtb = xb[0:m.np, m.slot, :]
                        m.xqb = xq[0:m.np, m.slot, :]
                        m.col = (lambda i, slot=m.slot, np_=m.np:
                                 stc[0:np_, i, slot:slot + 1])
                        ms.append(m)
                    wx = list(start_wait)
                    for m in ms:
                        if m.need_x is not None:
                            wx.append((dS, m.need_x))

                    def G(v):
                        v.tensor_copy(gapc[:, :], consts[:, 1:2])

                    def v1(v):
                        for m in ms:
                            v.reduce_sum(m.col(0), m.xbuf, axis=AX.X)
                    grp("vector", v1, w=wx)
                    s1 = grp("scalar", lambda s: [s.activation(
                        m.xlnb, m.xbuf, AF.Square, accum_out=m.col(1))
                        for m in ms], w=wx)

                    def v2(v):
                        for m in ms:
                            v.tensor_scalar_mul(m.col(2), m.col(0), 1.0 / 1024.0)
                            if solo:
                                G(v)
                        for m in ms:
                            v.tensor_scalar_mul(m.col(3), m.col(1), 1.0 / 1024.0)
                        for m in ms:
                            v.tensor_tensor(m.col(0), m.col(2), m.col(2), ALU.mult)
                            if solo:
                                G(v)
                        for m in ms:
                            v.tensor_tensor(m.col(3), m.col(3), m.col(0), ALU.subtract)
                            if solo:
                                G(v)
                        for m in ms:
                            v.tensor_scalar_add(m.col(3), m.col(3), 1e-5)
                    v2n = grp("vector", v2, w=[(cS, s1)])
                    s2n = grp("scalar", lambda s: [s.activation(
                        m.col(4), m.col(3), AF.Sqrt) for m in ms], w=[(cV, v2n)])

                    def v3(v):
                        for m in ms:
                            v.reciprocal(m.col(4), m.col(4))
                            if solo:
                                G(v)
                        for m in ms:
                            v.tensor_tensor(m.col(5), m.col(2), m.col(4), ALU.mult)
                            if solo:
                                G(v)
                        for m in ms:
                            v.tensor_scalar_mul(m.col(5), m.col(5), -1.0)
                    v3n = grp("vector", v3, w=[(cS, s2n)])
                    s3n = grp("scalar", lambda s: [s.activation(
                        m.xlnb, m.xbuf, AF.Identity, bias=m.col(5), scale=m.col(4))
                        for m in ms], w=[(cV, v3n)])

                    def v4(v):
                        for m in ms:
                            v.reduce_max(m.col(6), m.xlnb, axis=AX.X,
                                         apply_absolute_value=True)
                            if solo:
                                G(v)
                        for m in ms:
                            v.tensor_scalar_max(m.col(6), m.col(6), 1e-5)
                            if solo:
                                G(v)
                        for m in ms:
                            v.reciprocal(m.col(6), m.col(6))
                            if solo:
                                G(v)
                        for m in ms:
                            v.tensor_scalar_mul(m.col(6), m.col(6), 127.0)
                            if solo:
                                G(v)
                        for m in ms:
                            v.reciprocal(m.col(7), m.col(6))
                    v4n = grp("vector", v4, w=[(cS, s3n)])
                    s4n = grp("scalar", lambda s: [s.activation(
                        m.rtb, m.xlnb, AF.Identity, bias=magic[0:m.np, :],
                        scale=m.col(6)) for m in ms], w=[(cV, v4n)])
                    wv5 = [(cS, s4n)]
                    for m in ms:
                        if m.idx - 2 in tp_of:
                            wv5.append((cT, tp_of[m.idx - 2]))
                    if solo:
                        wv5.append((cV, cV.n))

                    def v5(v):
                        if solo:
                            G(v)
                        for m in ms:
                            v.tensor_scalar(m.xqb, m.rtb, -MAGIC, m.col(7),
                                            ALU.add, ALU.mult)
                    v5n = grp("vector", v5, w=wv5)
                    if _DBG and _DBG[0] == tag + "xq" and ms[0].idx == _DBG[1][0]:
                        dma("gpsimd", dG, dbg[:, 0:2 * D], xq[:, :, :], w=[(cV, v5n)])
                    if _DBG and _DBG[0] == tag + "xln" and ms[0].idx == _DBG[1][0]:
                        dma("gpsimd", dG, dbg[:, 0:2 * D],
                            xln[:, :, :].rearrange("p a b -> p (a b)"), w=[(cS, cS.n)])
                        dma("gpsimd", dG, dbg[:, 2 * D:2 * D + 16],
                            stc[:, :, :].rearrange("p a b -> p (a b)"), w=[(cS, cS.n)])
                    for m in ms:
                        v5_of[m.idx] = v5n
                    wtp = [(cV, v5n)]
                    for m in ms:
                        if m.idx - 2 in ev_of:
                            wtp.append((cS, ev_of[m.idx - 2]))

                    def tp(t_):
                        for m in ms:
                            q = xq[0:128, m.slot, :]
                            for d in range(DCH):
                                t_.transpose(pTb[:, m.slot, d * 128:(d + 1) * 128],
                                             q[:, d * 128:(d + 1) * 128], identb[:, :])
                    tpn = grp("tensor", tp, w=wtp)
                    for m in ms:
                        tp_of[m.idx] = tpn

                    def ev(s):
                        for m in ms:
                            if m.kind == "main":
                                co = tile_col(m.t)
                                dst = XT[:, :].rearrange("p (d c) -> p d c", c=nxc)[:, :, co:co + 128]
                                srcv = pTb[:, m.slot, :].rearrange("p (d c) -> p d c", c=128)
                            else:
                                dst = XT[:, :].rearrange("p (d c) -> p d c", c=nxc)[:, :, 0:nxc:nxc - 1]
                                srcv = pTb[:, m.slot, :].rearrange("p (d c) -> p d c", c=128)[:, :, 0:2]
                            s.copy(dst, srcv)
                    evn = grp("scalar", ev, w=[(cT, tpn)])
                    for m in ms:
                        ev_of[m.idx] = evn

                with nc.psum_tensor(tag + "pTb", [128, 2, D], BF16) as pTb:
                    todo = [(i, k, t) for i, (k, t) in enumerate(tiles)]
                    i = 0
                    while i < len(todo):
                        emit_quant(todo[i:i + 2])
                        i += 2
                quant_done = cS.n
                if _DBG and _DBG[0] == tag + "w1":
                    dma("gpsimd", dG, dbg[:, 0:DCH * C], w1sb[:, :],
                        w=[(dS, w1_in)])
                if _DBG and _DBG[0] == tag + "wq1d":
                    dma("gpsimd", dG,
                        dbg[:, 0:DCH * C].rearrange("p (k c) -> p k c", c=C),
                        wq1[:, :, :].rearrange("k p c -> p k c"),
                        w=[(dA, wq1_ready)])
                if _DBG and _DBG[0] == tag + "xt":
                    dma("gpsimd", dG, dbg[:, 0:DCH * nxc], XT[:, :],
                        w=[(cS, quant_done)])
                if stop == "quant":
                    return

                evac_of, gmul_of, rmax_of, ssl_of = {}, {}, {}, {}
                last_tpM = 0
                for p in range(NPAN if stop != "panel0" else 1):
                    base = panel_base(p)
                    for ct in range(CCH):
                        k = p * CCH + ct
                        slot = ct % 2

                        def mm1(t, base=base, ct=ct, slot=slot):
                            for d in range(DCH):
                                lhsT = w1sb[:, d * C + ct * 128: d * C + (ct + 1) * 128]
                                t.matmul(pHa[:, slot, :], lhsT,
                                         XT[:, d * nxc + base: d * nxc + base + 512],
                                         start=(d == 0), stop=(d == DCH - 1))
                            for d in range(DCH):
                                lhsT = w1sb[:, d * C + ct * 128: d * C + (ct + 1) * 128]
                                t.matmul(pHb[:, slot, 0:2], lhsT,
                                         XT[:, d * nxc + base + 512: d * nxc + base + 514],
                                         start=(d == 0), stop=(d == DCH - 1))
                        wmm = [(cS, quant_done), (dS, w1_in)] if k == 0 else []
                        if k >= 2:
                            wmm.append((cS, evac_of[k - 2]))
                        tmm = grp("tensor", mm1, w=wmm)

                        wev = [(cT, tmm)]
                        if p >= 1:
                            wev.append((cG, gmul_of[(p - 1) * CCH + ct]))

                        def evac(s, ct=ct, slot=slot):
                            s.activation(hp[:, ct, 0:512], pHa[:, slot, :],
                                         AF.Identity, scale=swi1[:, :])
                            s.activation(hp[:, ct, 512:514], pHb[:, slot, 0:2],
                                         AF.Identity, scale=swi1[:, :])
                        evac_of[k] = grp("scalar", evac, w=wev)
                        if _DBG and _DBG[0] == tag + "hp" and (p, ct) == _DBG[1]:
                            dma("gpsimd", dG, dbg[:, 0:PW + 2], hp[:, ct, :],
                                w=[(cS, evac_of[k])])

                        def conv(v, ct=ct):
                            v.tensor_scalar(cvt[:, :], hp[:, ct, 1:513],
                                            cw[:, 3 * ct + 1:3 * ct + 2], cb[:, ct:ct + 1],
                                            ALU.mult, ALU.add)
                            v.scalar_tensor_tensor(cvt[:, :], hp[:, ct, 0:512],
                                                   cw[:, 3 * ct:3 * ct + 1], cvt[:, :],
                                                   ALU.mult, ALU.add)
                            v.scalar_tensor_tensor(cvt[:, :], hp[:, ct, 2:514],
                                                   cw[:, 3 * ct + 2:3 * ct + 3], cvt[:, :],
                                                   ALU.mult, ALU.add)
                        vcv = grp("vector", conv, w=[(cS, evac_of[k])])
                        ssl_of[k] = grp("scalar", lambda s, ct=ct: s.activation(
                            hp[:, ct, 1:513], cvt[:, :], AF.Silu), w=[(cV, vcv)])
                        if _DBG and _DBG[0] == tag + "h2" and (p, ct) == _DBG[1]:
                            dma("gpsimd", dG, dbg[:, 0:PW + 2], hp[:, ct, :],
                                w=[(cS, ssl_of[k])])

                    wrm = [(cS, ssl_of[p * CCH + CCH - 1])]
                    if p >= 1:
                        wrm.append((cT, last_tpM))
                    rmax_of[p] = grp("vector", lambda v: v.reduce_max(
                        Rmx[:, :], hp[:, :, 1:513].rearrange("p c t -> p t c"),
                        axis=AX.X, apply_absolute_value=True), w=wrm)

                    evS_of = {}
                    for sp in range(2):
                        mm_ = [2 * sp, 2 * sp + 1]
                        wtM = [(cV, rmax_of[p])]
                        if sp >= 1:
                            wtM.append((cS, evS_of[0]))
                        tM = grp("tensor", lambda t, mm_=mm_: [t.transpose(
                            pT[:, s4 % 2, :], Rmx[:, s4 * 128:(s4 + 1) * 128],
                            identf[:, :]) for s4 in mm_], w=wtM)
                        sM = grp("scalar", lambda s, mm_=mm_: [s.copy(
                            Mt[:, s4 % 2, :], pT[:, s4 % 2, :]) for s4 in mm_],
                            w=[(cT, tM)])

                        def vM(v, mm_=mm_, p=p):
                            c0 = lambda s4: Mt[:, s4 % 2, 0:1]
                            for s4 in mm_:
                                v.reduce_max(c0(s4), Mt[:, s4 % 2, :], axis=AX.X)
                            for s4 in mm_:
                                v.tensor_scalar_max(c0(s4), c0(s4), 1e-5)
                            for s4 in mm_:
                                v.reciprocal(c0(s4), c0(s4))
                            for s4 in mm_:
                                v.tensor_scalar_mul(c0(s4), c0(s4), 127.0)
                            for s4 in mm_:
                                v.reciprocal(r2cols[:, p * 4 + s4:p * 4 + s4 + 1], c0(s4))
                        vMn = grp("vector", vM, w=[(cS, sM)])
                        tS = grp("tensor", lambda t, mm_=mm_: [t.transpose(
                            pT[0:1, s4 % 2, 0:128], Mt[:, s4 % 2, 0:1], identf[:, :])
                            for s4 in mm_], w=[(cV, vMn)])
                        evS_of[sp] = grp("scalar", lambda s, mm_=mm_: [s.copy(
                            browbuf[0:1, s4 * 128:(s4 + 1) * 128], pT[0:1, s4 % 2, 0:128])
                            for s4 in mm_], w=[(cT, tS)])
                    wB = [(cS, evS_of[1])]
                    if p >= 1:
                        wB.append((cG, gmul_of[p * CCH - 1]))
                    tB = grp("tensor", lambda t: t.matmul(
                        pB[:, :], ones128[:, :], browbuf[:, 0:512], start=True, stop=True), w=wB)
                    sB = grp("scalar", lambda s: s.copy(S2B[:, :], pB[:, :]), w=[(cT, tB)])
                    vr_of = {}
                    for ct in range(CCH):
                        k = p * CCH + ct
                        par = ct % 2
                        wg = [(cS, sB)]
                        if ct >= 2:
                            wg.append((cV, vr_of[ct - 2]))
                        gmul_of[k] = grp("gpsimd", lambda g, ct=ct, par=par: g.tensor_tensor(
                            rnd[:, par, :], hp[:, ct, 1:513], S2B[:, :], ALU.mult), w=wg)
                        wv = [(cG, gmul_of[k])]
                        if ct >= 2:
                            wv.append((dA, dA.n - 16))
                        vr_of[ct] = grp("vector", lambda v, par=par: v.tensor_scalar(
                            q2s[:, par, :], rnd[:, par, :], MAGIC, -MAGIC, ALU.add, ALU.add),
                            w=wv)
                        if _DBG and _DBG[0] == tag + "q2s" and (p, ct) == _DBG[1]:
                            dma("gpsimd", dG, dbg[:, 0:PW], q2s[:, par, :],
                                w=[(cV, vr_of[ct])])
                        dma("scalar", dA, x2q[4 * p:4 * p + 4, :, ct, :].rearrange("b p c -> p b c"),
                            q2s[:, par, :].rearrange("p (b c) -> p b c", c=128),
                            w=[(cV, vr_of[ct])])
                    if _DBG and _DBG[0] == tag + "r2" and p == NPAN - 1:
                        dma("gpsimd", dG, dbg[:, 0:NT], r2cols[:, :], w=[(cV, cV.n)])
                        dma("gpsimd", dG, dbg[:, 32:32 + PW], S2B[:, :], w=[(cV, cV.n)])
                    last_tpM = cT.n

        def ffn_sweep2(tag, res_src, x2q_ready, out_sb=None, store_dram=None, tap=None):
            with nc.sbuf_tensor(tag + "w2sb", [128, CCH * D], BF16) as w2sb, \
                 nc.sbuf_tensor(tag + "lhs", [128, 2, C], BF16) as lhsb, \
                 nc.sbuf_tensor(tag + "xr", [128, 2, D], F32) as xrb, \
                 nc.sbuf_tensor(tag + "og", [128, 2, D], F32) as og, \
                 nc.psum_tensor(tag + "p2", [128, 2, D], F32) as p2:
                for k in range(CCH):
                    w2_in = dma("sync", dS, w2sb[:, k * D:(k + 1) * D],
                                wq2[k, :, :],
                                w=[(dA, max(wq2_ready, x2q_ready))] if k == 0 else [])
                def _mks(v):
                    v.tensor_scalar(sccols[:, :], r2cols[:, :], sch2[:, :], None, ALU.mult)
                    v.tensor_copy(gapc[:, :], consts[:, 1:2])
                    v.tensor_copy(gapc[:, :], consts[:, 1:2])
                scd = grp("vector", _mks, w=[])
                mm_of, ev_of, st_of = {}, {}, {}
                for tt in range(NT):
                    slot = tt % 2
                    wl = [(dA, x2q_ready)]
                    if tt >= 2:
                        wl.append((cT, mm_of[tt - 2]))
                        wl.append((cV, ev_of[tt - 2]))
                    n1 = dma("sync", dS, lhsb[:, slot, :],
                             x2q[tt, :, :, :].rearrange("p b c -> p (b c)"), w=wl)
                    n2 = dma("sync", dS, xrb[:, slot, :], res_src(tt))

                    def mm2(t, slot=slot):
                        for cc in range(CCH):
                            lhsT = lhsb[:, slot, cc * 128:(cc + 1) * 128]
                            for dh in range(2):
                                t.matmul(p2[:, slot, dh * 512:(dh + 1) * 512], lhsT,
                                         w2sb[:, cc * D + dh * 512: cc * D + (dh + 1) * 512],
                                         start=(cc == 0), stop=(cc == CCH - 1))
                    wm = [(dS, n2)]
                    if tt == 0:
                        wm.append((dS, w2_in))
                    if tt >= 2:
                        wm.append((cV, ev_of[tt - 2]))
                    mm_of[tt] = grp("tensor", mm2, w=wm)
                    dstt = og[:, slot, :] if store_dram is not None else out_sb(tt)
                    wev = [(cT, mm_of[tt]), (cV, scd)]
                    if store_dram is not None and tt >= 2:
                        wev.append((dA, st_of[tt - 2]))

                    def ev2(v, slot=slot, tt=tt, dstt=dstt):
                        v.scalar_tensor_tensor(dstt, p2[:, slot, :], sccols[:, tt:tt + 1],
                                               xrb[:, slot, :], ALU.mult, ALU.add)
                    ev_of[tt] = grp("vector", ev2, w=wev)
                    if store_dram is not None:
                        st_of[tt] = dma("scalar", dA, store_dram[tt * 128:(tt + 1) * 128, :],
                                        og[:, slot, :], w=[(cV, ev_of[tt])])
                    if tap is not None:
                        dma("scalar", dA, tap[tt * 128:(tt + 1) * 128, :],
                            out_sb(tt), w=[(cV, ev_of[tt])])
                return dict(ev_of=ev_of)

        ph0_bar = [(dA, wq2_ready), (cS, cS.n), (cV, cV.n)]
        ffn_sweep1("f1", NXC1, NXT1, lambda t: x_in[t * 128:(t + 1) * 128, :],
                   lambda t: [], lambda t: 128 * t, False, lambda p: 127 + 512 * p,
                   start_wait=ph0_bar,
                   stop={"f1quant": "quant", "f1p0": "panel0"}.get(stop_after))
        if stop_after in ("f1quant", "f1p0"):
            return nc
        x2q_f1 = dA.n
        if stop_after == "f1s1":
            return nc

        with nc.sbuf_tensor("XR", [128, NT * D], F32) as XR:
            ffn_sweep2("f1", lambda tt: x_in[128 + tt * 128: 256 + tt * 128, :],
                       x2q_f1, out_sb=lambda tt: XR[:, tt * D:(tt + 1) * D],
                       tap=tap_aps.get("x1"))
            ffn1_done = cV.n
            if stop_after == "f1s2":
                return nc

            with nc.sbuf_tensor("QT", [128, DCH * TC], BF16) as QT:
                with nc.sbuf_tensor("wkv", [128, DCH * 3 * D], BF16) as wkv, \
                     nc.sbuf_tensor("x2T", [128, DCH * 512], BF16) as x2T, \
                     nc.sbuf_tensor("Kst", [128, 8 * 512], BF16) as Kst, \
                     nc.sbuf_tensor("Vst", [128, 2, D], BF16) as Vst, \
                     nc.sbuf_tensor("Bv", [128, D], F32) as Bv, \
                     nc.sbuf_tensor("qst", [128, 8, 2], F32) as qst, \
                     nc.sbuf_tensor("sq2", [128, D], F32) as sq2, \
                     nc.sbuf_tensor("qxln", [128, 2, D], BF16) as qxln, \
                     nc.psum_tensor("pQ", [128, 4, 512], F32) as pQ, \
                     nc.psum_tensor("pTb2", [128, 2, D], BF16) as pTb2:
                    nbv = dma("gpsimd", dG, browbuf[0:1, :], bvrow_d,
                              w=[(cV, ffn1_done)])
                    for k in range(DCH):
                        nwk = dma("gpsimd", dG, wkv[:, k * 3 * D:(k + 1) * 3 * D],
                                  wqkvT[k * 128:(k + 1) * 128, :])
                    tbv = grp("tensor", lambda t: [t.matmul(pQ[:, h, :], ones128[:, :],
                                                            browbuf[:, h * 512:(h + 1) * 512],
                                                            start=True, stop=True)
                                                   for h in range(2)], w=[(dG, nbv), (cV, bb_zero)])
                    sbv = grp("scalar", lambda s: [s.copy(Bv[:, 0:512], pQ[:, 0, :]),
                                                   s.copy(Bv[:, 512:1024], pQ[:, 1, :])],
                              w=[(cT, tbv)])
                    qk_mm_last = 0
                    tn_of = {}
                    nk_of = {}
                    for ch in range(4):
                        ev_last = 0
                        for pr in range(2):
                            q4s = [2 * pr, 2 * pr + 1]
                            tts = [ch * 4 + q4 for q4 in q4s]
                            cols = [(lambda i, slot=q4 % 2:
                                     qst[:, i, slot:slot + 1]) for q4 in q4s]
                            xss = [XR[:, tt * D:(tt + 1) * D] for tt in tts]
                            wfirst = [(cV, ffn1_done)] if tts[0] == 0 else []
                            v1 = grp("vector", lambda v: [v.reduce_sum(
                                c(0), xs, axis=AX.X) for xs, c in zip(xss, cols)],
                                w=wfirst)
                            s1 = grp("scalar", lambda s: [s.activation(
                                sq2[:, :], xs, AF.Square, accum_out=c(1))
                                for xs, c in zip(xss, cols)], w=wfirst)

                            def v2(v):
                                for c in cols:
                                    v.tensor_scalar_mul(c(2), c(0), 1.0 / 1024.0)
                                for c in cols:
                                    v.tensor_scalar_mul(c(3), c(1), 1.0 / 1024.0)
                                for c in cols:
                                    v.tensor_tensor(c(0), c(2), c(2), ALU.mult)
                                for c in cols:
                                    v.tensor_tensor(c(3), c(3), c(0), ALU.subtract)
                                for c in cols:
                                    v.tensor_scalar_add(c(3), c(3), 1e-5)
                            v2n = grp("vector", v2, w=[(cS, s1)])
                            s2n = grp("scalar", lambda s: [s.activation(
                                c(4), c(3), AF.Sqrt) for c in cols], w=[(cV, v2n)])

                            def v3(v):
                                for c in cols:
                                    v.reciprocal(c(4), c(4))
                                for c in cols:
                                    v.tensor_tensor(c(5), c(2), c(4), ALU.mult)
                                for c in cols:
                                    v.tensor_scalar_mul(c(5), c(5), -1.0)
                            v3n = grp("vector", v3, w=[(cS, s2n)])
                            ws3 = [(cV, v3n)]
                            for tt in tts:
                                if tt - 2 in tn_of:
                                    ws3.append((cT, tn_of[tt - 2]))
                            s3n = grp("scalar", lambda s: [s.activation(
                                qxln[:, q4 % 2, :], xs, AF.Identity,
                                bias=c(5), scale=c(4))
                                for q4, xs, c in zip(q4s, xss, cols)], w=ws3)
                            wtp = [(cS, s3n)]
                            if ev_last:
                                wtp.append((cS, ev_last))
                            if ch > 0 and pr == 0:
                                wtp.append((cT, qk_mm_last))

                            def tp(t):
                                for q4 in q4s:
                                    o = qxln[:, q4 % 2, :]
                                    for d in range(DCH):
                                        t.transpose(pTb2[:, q4 % 2, d * 128:(d + 1) * 128],
                                                    o[:, d * 128:(d + 1) * 128], identb[:, :])
                            tpn = grp("tensor", tp, w=wtp)
                            for tt in tts:
                                tn_of[tt] = tpn

                            def ev(s):
                                for q4 in q4s:
                                    dst = x2T[:, :].rearrange("p (d c) -> p d c", c=512)[:, :, q4 * 128:(q4 + 1) * 128]
                                    srcv = pTb2[:, q4 % 2, :].rearrange("p (d c) -> p d c", c=128)
                                    s.copy(dst, srcv)
                            ev_last = grp("scalar", ev, w=[(cT, tpn)])
                        x2T_done = ev_last
                        qk_ev = {}
                        for ft in range(16):
                            def mmq(t, ft=ft):
                                for d in range(DCH):
                                    t.matmul(pQ[:, ft % 4, :],
                                             wkv[:, d * 3 * D + ft * 128: d * 3 * D + (ft + 1) * 128],
                                             x2T[:, d * 512:(d + 1) * 512],
                                             start=(d == 0), stop=(d == DCH - 1))
                            wq_ = []
                            if ft == 0:
                                wq_ = [(cS, x2T_done), (dG, nwk), (cS, sbv)]
                            if ft >= 4:
                                wq_.append((cS, qk_ev[ft - 4]))
                            tq = grp("tensor", mmq, w=wq_)
                            if ft < 8:
                                dst = QT[:, ft * TC + ch * 512: ft * TC + (ch + 1) * 512]
                            else:
                                dst = Kst[:, :].rearrange(
                                    "p (jl d c) -> p jl d c", d=8, c=128)[:, :, ft - 8, :]
                            wqe = [(cT, tq)]
                            if ft >= 8 and (ch - 1) in nk_of:
                                wqe.append((dA, nk_of[ch - 1]))
                            qk_ev[ft] = grp("scalar", lambda s, dst=dst, ft=ft: s.activation(
                                dst, pQ[:, ft % 4, :], AF.Identity, bias=bcol[:, ft:ft + 1]),
                                w=wqe)
                        nk_of[ch] = dma(
                            "scalar", dA,
                            kv_in[0, 0:VBASE].rearrange("(p jt dc) -> p jt dc", p=128, dc=1024)[:, ch * 4:(ch + 1) * 4, :],
                            Kst[:, :].rearrange("p (jl dc) -> p jl dc", dc=1024),
                            w=[(cS, qk_ev[15])])
                        v_ev = {}
                        nv_of = {}
                        for t4 in range(4):
                            def mmv(t, t4=t4):
                                for d in range(DCH):
                                    for dh in range(2):
                                        t.matmul(pQ[:, (t4 % 2) * 2 + dh, :],
                                                 x2T[:, d * 512 + t4 * 128: d * 512 + (t4 + 1) * 128],
                                                 wkv[:, d * 3 * D + 2 * D + dh * 512: d * 3 * D + 2 * D + (dh + 1) * 512],
                                                 start=(d == 0), stop=(d == DCH - 1))
                            wv_ = [(cS, qk_ev[15])] if t4 in (0, 1) else [(cV, v_ev[t4 - 2])]
                            tv = grp("tensor", mmv, w=wv_)
                            wve = [(cT, tv)]
                            if t4 >= 2:
                                wve.append((dA, nv_of[t4 - 2]))
                            v_ev[t4] = grp("vector", lambda v, t4=t4: v.scalar_tensor_tensor(
                                Vst[:, t4 % 2, :], pQ[:, (t4 % 2) * 2:(t4 % 2) * 2 + 2, :],
                                1.0, Bv[:, :], ALU.mult, ALU.add), w=wve)
                            tl = ch * 4 + t4
                            nv_of[t4] = dma(
                                "scalar", dA,
                                kv_in[0, VBASE + tl * 128 * D: VBASE + (tl + 1) * 128 * D].rearrange(
                                    "(p c) -> p c", c=D),
                                Vst[:, t4 % 2, :], w=[(cV, v_ev[t4])])
                        qk_mm_last = cT.n
                    kv_stores = dA.n
                    if _DBG and _DBG[0] == "aq":
                        dma("gpsimd", dG, dbg[:, 0:DCH * TC], QT[:, :],
                            w=[(cS, cS.n)])
                    if _DBG and _DBG[0] == "akvin":
                        dma("gpsimd", dG, dbg[:, 0:KVN // 128].rearrange(
                            "p n -> (p n)")[None, :], kv_in[:, :], w=[(dA, kv_stores)])
                    gkv = grp("gpsimd", lambda g: g.collective_compute(
                        "AllGather", ALU.bypass,
                        replica_groups=[[0, 1, 2, 3], [4, 5, 6, 7]],
                        ins=[kv_in], outs=[kv_out]),
                        w=[(dA, kv_stores)], ctr=cC)
                    if stop_after == "qkv":
                        return nc

                with nc.sbuf_tensor("Kbuf", [128, 2, DCH * 128], BF16) as Kbuf, \
                     nc.sbuf_tensor("Pt", [128, 2, TC], BF16) as Pt, \
                     nc.sbuf_tensor("Zac", [128, 4], F32) as Zac, \
                     nc.psum_tensor("pA", [128, 8, 512], F32) as pA:
                    mm_of, ex_of, zr_of, pd_of = {}, {}, {}, {}
                    for jt in range(JT):
                        r, lt = jt // 16, jt % 16
                        par = jt % 2
                        wk = [(cC, gkv)] if jt == 0 else []
                        if jt >= 2:
                            wk.append((cT, mm_of[jt - 2]))
                        nk = dma("sync", dS, Kbuf[:, par, :],
                                 kv_out[r, 0:VBASE].rearrange("(p jt dc) -> p jt dc", p=128, dc=1024)[:, lt, :],
                                 w=wk)

                        def mma(t, par=par):
                            for d in range(DCH):
                                for ic in range(4):
                                    t.matmul(pA[:, par * 4 + ic, :],
                                             Kbuf[:, par, d * 128:(d + 1) * 128],
                                             QT[:, d * TC + ic * 512: d * TC + (ic + 1) * 512],
                                             start=(d == 0), stop=(d == DCH - 1))
                        wm = [(dS, nk)]
                        if jt >= 2:
                            wm.append((cS, ex_of[jt - 2]))
                        mm_of[jt] = grp("tensor", mma, w=wm)
                        wex = [(cT, mm_of[jt])]
                        if jt >= 1:
                            wex.append((cV, zr_of[jt - 1]))
                        if jt >= 2:
                            wex.append((dA, pd_of[jt - 2]))

                        def exp_(s, par=par):
                            for ic in range(4):
                                s.activation(Pt[:, par, ic * 512:(ic + 1) * 512],
                                             pA[:, par * 4 + ic, :], AF.Exp,
                                             scale=1.0 / 1024.0,
                                             accum_out=Zac[:, ic:ic + 1])
                        ex_of[jt] = grp("scalar", exp_, w=wex)
                        zr_of[jt] = grp("vector", lambda v, jt=jt: v.reduce_sum(
                            Zbuf[:, jt:jt + 1], Zac[:, 0:4], axis=AX.X), w=[(cS, ex_of[jt])])
                        pd_of[jt] = dma("scalar", dA,
                                        pst[jt, :, :, :, :].rearrange("g p b c -> p g (b c)"),
                                        Pt[:, par, :].rearrange("p (g bc) -> p g bc", bc=512),
                                        w=[(cS, ex_of[jt])])
                    if _DBG and _DBG[0] == "az":
                        dma("gpsimd", dG, dbg[:, 0:JT], Zbuf[:, :],
                            w=[(cV, zr_of[JT - 1])])
                    if _DBG and _DBG[0] == "ap":
                        jd = _DBG[1][0]
                        dma("gpsimd", dG, dbg[:, 0:TC], Pt[:, jd % 2, :],
                            w=[(cS, ex_of[jd])])
                    nzi = dma("gpsimd", dG, z_in[0, :].rearrange("(p j) -> p j", j=JT),
                              Zbuf[:, :], w=[(cV, zr_of[JT - 1])])
                    gz = grp("gpsimd", lambda g: g.collective_compute(
                        "AllReduce", ALU.add,
                        replica_groups=[[0, 1, 2, 3], [4, 5, 6, 7]],
                        ins=[z_in], outs=[z_out]), w=[(dG, nzi)], ctr=cC)
                    nzo = dma("gpsimd", dG, Zbuf[:, :],
                              z_out[0, :].rearrange("(p j) -> p j", j=JT), w=[(cC, gz)])
                    def _rz(v):
                        v.reciprocal(Zbuf[:, :], Zbuf[:, :])
                        v.tensor_copy(gapc[:, :], consts[:, 1:2])
                        v.tensor_copy(gapc[:, :], consts[:, 1:2])
                    rz_done = grp("vector", _rz, w=[(dG, nzo)])
                p_dma_all = dA.n
                if stop_after == "apass":
                    return nc

            spill_of = {}
            with nc.sbuf_tensor("U", [128, JT * D], BF16) as U, \
                 nc.sbuf_tensor("vstg", [128, 2, D], BF16) as vstg, \
                 nc.sbuf_tensor("Pblk", [128, 10, 256], BF16) as Pblk, \
                 nc.psum_tensor("pO", [128, 8, 512], F32) as pO:
                us_of = {}
                for jt in range(JT):
                    r, lt = jt // 16, jt % 16
                    wu = [(cV, rz_done), (dA, p_dma_all)] if jt == 0 else []
                    if jt >= 2:
                        wu.append((cV, us_of[jt - 2]))
                    nu = dma("sync", dS, vstg[:, jt % 2, :],
                             kv_out[r, VBASE + lt * 128 * D: VBASE + (lt + 1) * 128 * D].rearrange(
                                 "(p c) -> p c", c=D), w=wu)
                    if _DBG and _DBG[0] == "azr" and jt == 0:
                        dma("gpsimd", dG, dbg[:, 0:JT], Zbuf[:, :], w=[(cV, rz_done)])
                    us_of[jt] = grp("vector", lambda v, jt=jt: v.tensor_scalar(
                        U[:, jt * D:(jt + 1) * D], vstg[:, jt % 2, :],
                        Zbuf[:, jt:jt + 1], None, ALU.mult), w=[(dS, nu)])
                us_gap = grp("vector", lambda v: [v.tensor_copy(gapc[:, :], consts[:, 1:2]),
                                                  v.tensor_copy(gapc[:, :], consts[:, 1:2])],
                             w=[])
                if _DBG and _DBG[0] == "au":
                    dma("gpsimd", dG, dbg[:, 0:8 * D], U[:, 0:8 * D],
                        w=[(cV, us_of[7])])
                for ip in range(8):
                    dr_of = {}
                    bu_of = {}
                    for blk in range(8):
                        st_ = blk % 2
                        lds = []
                        for jj in range(8):
                            jt = blk * 8 + jj
                            wp = []
                            if ip == 0 and blk == 0 and jj == 0:
                                wp.append((dA, p_dma_all))
                            prev = (jt - 10) // 8
                            if prev >= 0:
                                wp.append((cT, bu_of[prev]))
                            elif ip > 0 and jj == 0 and blk == 0:
                                wp.append((cT, cT.n))
                            lds.append(dma(
                                "sync", dS, Pblk[:, jt % 10, :],
                                pst[jt, ip // 2, :, (ip % 2) * 2:(ip % 2) * 2 + 2, :].rearrange(
                                    "p b c -> p (b c)"), w=wp))

                        def burst(t, blk=blk, st_=st_):
                            for jj in range(8):
                                jt = blk * 8 + jj
                                for ti in range(2):
                                    for dh in range(2):
                                        t.matmul(
                                            pO[:, st_ * 4 + ti * 2 + dh, :],
                                            Pblk[:, jt % 10, ti * 128:(ti + 1) * 128],
                                            U[:, jt * D + dh * 512: jt * D + (dh + 1) * 512],
                                            start=(jj == 0), stop=(jj == 7))
                        wb = [(dS, lds[-1]),
                              (cV, us_of[blk * 8 + 15] if blk < 7 else us_gap)]
                        if blk >= 2:
                            wb.append((cV, dr_of[blk - 2]))
                        if blk < 2 and ip > 0:
                            wb.append((cV, dr_of_prev[blk - 2 + 8]))
                        bu_of[blk] = grp("tensor", burst, w=wb)

                        def drain(v, blk=blk, st_=st_, ip=ip):
                            for ti in range(2):
                                tt = ip * 2 + ti
                                v.tensor_tensor(
                                    XR[:, tt * D:(tt + 1) * D],
                                    pO[:, st_ * 4 + ti * 2:st_ * 4 + ti * 2 + 2, :],
                                    XR[:, tt * D:(tt + 1) * D], ALU.add)
                        dr_of[blk] = grp("vector", drain, w=[(cT, bu_of[blk])])
                        if _DBG and _DBG[0] == "axr0" and ip == 0 and blk == _DBG[1][0]:
                            dma("gpsimd", dG, dbg[:, 0:2 * D], XR[:, 0:2 * D],
                                w=[(cV, dr_of[blk])])
                        if _DBG and _DBG[0] == "au2" and ip == 0 and blk == _DBG[1][0]:
                            dma("gpsimd", dG, dbg[:, 0:8 * D],
                                U[:, (blk * 8) * D:(blk * 8 + 8) * D],
                                w=[(cV, dr_of[blk])])
                    dr_of_prev = dr_of
                    for ti in range(2):
                        tt = ip * 2 + ti
                        spill_of[tt] = dma("scalar", dA, x2d[tt * 128:(tt + 1) * 128, :],
                                           XR[:, tt * D:(tt + 1) * D],
                                           w=[(cV, dr_of[7])])
                        if "x2" in tap_aps:
                            dma("scalar", dA, tap_aps["x2"][tt * 128:(tt + 1) * 128, :],
                                XR[:, tt * D:(tt + 1) * D], w=[(cV, dr_of[7])])
            n1 = dma("gpsimd", dG, hx_in[0:1, :], XR[0:1, 0:D], w=[(dA, spill_of[0])])
            n2 = dma("gpsimd", dG, hx_in[1:2, :], XR[127:128, 15 * D:16 * D],
                     w=[(dA, spill_of[15])])
            gh = grp("gpsimd", lambda g: g.collective_compute(
                "AllGather", ALU.bypass,
                replica_groups=[[0, 1, 2, 3, 4, 5, 6, 7]],
                ins=[hx_in], outs=[hx_out]), w=[(dG, n2)], ctr=cC)

        if stop_after == "opass":
            return nc
        xm2 = ctx.enter_context(nc.sbuf_tensor("xm2_s", [2, D], F32))
        HOLD["xm2"] = xm2
        with nc.sbuf_tensor("hxg", [128, D], F32) as hxg, \
             nc.psum_tensor("pH", [2, D], F32) as pH:
            hz = grp("vector", lambda v: v.memset(hxg[:, :], 0.0), w=[(dG, n2)])
            nh = dma("gpsimd", dG, hxg[0:16, :], hx_out[:, :], w=[(cC, gh), (cV, hz)])
            th = grp("tensor", lambda t: [t.matmul(pH[0:2, h * 512:(h + 1) * 512],
                                                   selmat[:, :],
                                                   hxg[:, h * 512:(h + 1) * 512],
                                                   start=True, stop=True)
                                          for h in range(2)], w=[(dG, nh)])
            halo_done = grp("scalar", lambda s: s.copy(xm2[0:2, :], pH[0:2, :]),
                            w=[(cT, th)])

        ffn_sweep1("f2", NXC2, NT, lambda t: x2d[t * 128:(t + 1) * 128, :],
                   lambda t: [(dA, spill_of[t])], lambda t: 1 + 128 * t, True,
                   lambda p: 512 * p, start_wait=[(cS, halo_done)])
        x2q_f2 = dA.n
        if stop_after == "f2s1":
            return nc
        ffn_sweep2("f2", lambda tt: x2d[tt * 128:(tt + 1) * 128, :],
                   x2q_f2, store_dram=y_out)

        def fin(y):
            y.wait_ge(dA.sem, dA.n)
            y.wait_ge(dS.sem, dS.n)
            y.wait_ge(dG.sem, dG.n)
            y.wait_ge(cC.sem, cC.n)
        grp("sync", fin, w=[])

    return nc


_NC_CACHE = {}


def _get_nc(taps):
    key = (tuple(sorted(taps)), repr(_DBG))
    if key not in _NC_CACHE:
        _NC_CACHE[key] = _build(taps)
    return _NC_CACHE[key]


def kernel(**inputs):
    f32 = np.float32
    x = np.asarray(inputs["L0_attn_x"], dtype=f32)
    W = np.asarray(inputs["L0_attn_W"], dtype=f32)
    b = np.asarray(inputs["L0_attn_b"], dtype=f32)
    w1 = np.asarray(inputs["L0_ffn_w1"], dtype=f32)
    cwf = np.asarray(inputs["L0_ffn_conv_w"], dtype=f32)
    cbf = np.asarray(inputs["L0_ffn_conv_b"], dtype=f32)
    w2 = np.asarray(inputs["L0_ffn_w2"], dtype=f32)

    w1T = np.ascontiguousarray(w1.T)
    w2T = np.ascontiguousarray(w2.T)
    wqkvT = np.ascontiguousarray(W.T)
    bcol = np.ascontiguousarray(b.reshape(24, 128).T)
    bvrow = np.ascontiguousarray(b[2048:3072].reshape(1, D))
    cwcol = np.ascontiguousarray(
        cwf[:, 0, :].reshape(32, 128, 3).transpose(1, 0, 2).reshape(128, 96))
    cbcol = np.ascontiguousarray(cbf.reshape(32, 128).T)
    identb = np.eye(128, dtype=ml_dtypes.bfloat16)
    identf = np.eye(128, dtype=f32)
    ones128 = np.ones((128, 128), dtype=f32)
    consts = np.tile(np.array([[MAGIC, -MAGIC, 0.0, 0.0]], dtype=f32), (128, 1))

    in_maps = []
    for c in range(8):
        s, r0 = c // 4, (c % 4) * TC
        xsl = np.zeros((NXC1, D), dtype=f32)
        lo, hi = r0 - 128, r0 + TC + 128
        slo, shi = max(lo, 0), min(hi, SEQ)
        xsl[slo - lo: shi - lo] = x[s, slo:shi]
        sel = np.zeros((128, 2), dtype=f32)
        if c % 4 != 0:
            sel[2 * (c - 1) + 1, 0] = 1.0
        if c % 4 != 3:
            sel[2 * (c + 1), 1] = 1.0
        in_maps.append(dict(
            x_in=np.ascontiguousarray(xsl), w1T=w1T, w2T=w2T, wqkvT=wqkvT,
            bcol=bcol, bvrow=bvrow, cwcol=cwcol, cbcol=cbcol,
            identb=identb, identf=identf, ones128=ones128,
            selmat=np.ascontiguousarray(sel), consts=consts))

    nc = _get_nc(_TAPS)
    res = run_bass_kernel_spmd(nc, in_maps, core_ids=list(range(8)))
    y = np.empty((2, SEQ, D), dtype=f32)
    for c in range(8):
        s, r0 = c // 4, (c % 4) * TC
        y[s, r0:r0 + TC] = res.results[c]["y_out"]
    if _TAPS:
        kernel._taps = [{t: res.results[c]["tap_" + t] for t in _TAPS}
                        for c in range(8)]
    if _DBG:
        kernel._dbg = [res.results[c]["dbg"] for c in range(8)]
    return y
